# version 02
# baseline (speedup 1.0000x reference)
import numpy as np
import ml_dtypes
from contextlib import ExitStack

import concourse.bass as bass
import concourse.mybir as mybir
from concourse.bass_utils import run_bass_kernel_spmd

F32 = mybir.dt.float32
BF16 = mybir.dt.bfloat16
AF = mybir.ActivationFunctionType
ALU = mybir.AluOpType
AX = mybir.AxisListType.X
BF = ml_dtypes.bfloat16

NCORES = 8
T = 8192
D = 4096
TL = T // NCORES
DSL = D // NCORES
NST = 16
RK = 128
HEADS = 32
HD = 128
EPS = 1e-5
Q_START, Q_END = 611, 8191
NQTOK = float(Q_END - Q_START)


class Prog:

    def __init__(self, nc, stack):
        self.nc = nc
        self.stack = stack
        self.phases = []
        self.bar = stack.enter_context(nc.semaphore("BAR"))
        self._sems = {}

    def sem(self, name):
        if name not in self._sems:
            self._sems[name] = self.stack.enter_context(self.nc.semaphore(name))
        return self._sems[name]

    def phase(self, **fns):
        self.phases.append(fns)

    def finalize(self):
        def runner(eng_name):
            def run(eng):
                for i, ph in enumerate(self.phases):
                    if eng_name in ph:
                        ph[eng_name](eng)
                    eng.sem_inc(self.bar, 1)
                    eng.wait_ge(self.bar, 5 * (i + 1))
            return run

        with self.nc.Block() as block:
            block.sync(runner("sync"))
            block.tensor(runner("tensor"))
            block.vector(runner("vector"))
            block.scalar(runner("scalar"))
            block.gpsimd(runner("gpsimd"))


def _prep(inputs):
    d = {}
    x = np.asarray(inputs["L0_attn_x"])
    d["x_full"] = np.ascontiguousarray(x.reshape(T, D))
    d["xpwT"] = np.ascontiguousarray(np.asarray(inputs["L0_attn_x_proj_w"]).T)
    A_log = np.asarray(inputs["L0_attn_A_log"])
    Dvec = np.asarray(inputs["L0_attn_D"])
    dtw = np.asarray(inputs["L0_attn_dt_proj_w"])
    dtb = np.asarray(inputs["L0_attn_dt_proj_b"])
    d["Alog_sl"] = []
    d["D_sl"] = []
    d["dtpwT_sl"] = []
    d["dtpb_sl"] = []
    d["x_cols"] = []
    for c in range(NCORES):
        sl = slice(c * DSL, (c + 1) * DSL)
        a = A_log[sl].reshape(4, 128, NST).transpose(1, 0, 2).reshape(128, 4 * NST)
        d["Alog_sl"].append(np.ascontiguousarray(a))
        d["D_sl"].append(np.ascontiguousarray(Dvec[sl].reshape(4, 128).T))
        d["dtpb_sl"].append(np.ascontiguousarray(dtb[sl].reshape(4, 128).T))
        d["dtpwT_sl"].append(np.ascontiguousarray(dtw[sl].T))
        d["x_cols"].append(np.ascontiguousarray(d["x_full"][:, sl]))
    d["W0T"] = np.ascontiguousarray(np.asarray(inputs["L0_ffn_W"]).T)
    d["b0"] = np.asarray(inputs["L0_ffn_b"]).reshape(1, D)
    A0 = np.asarray(inputs["L0_ffn_A"])
    d["A0catT"] = np.ascontiguousarray(A0.reshape(128, D).T)
    B0 = np.asarray(inputs["L0_ffn_B"])
    d["B0flatT"] = np.ascontiguousarray(B0.transpose(0, 2, 1).reshape(128, D))
    d["rWT"] = np.ascontiguousarray(np.asarray(inputs["L0_ffn_router_W"]).T)
    d["rb"] = np.ascontiguousarray(np.asarray(inputs["L0_ffn_router_b"]).reshape(8, 1))
    d["qkvw"] = np.asarray(inputs["L1_attn_attn_qkvw"])
    d["qkvb"] = np.asarray(inputs["L1_attn_attn_qkvb"]).reshape(1, 3 * D)
    d["ow"] = np.asarray(inputs["L1_attn_attn_ow"])
    d["ob"] = np.asarray(inputs["L1_attn_attn_ob"]).reshape(1, D)
    d["r1WT"] = np.ascontiguousarray(np.asarray(inputs["L1_ffn_route_w"]).T)
    A1 = np.asarray(inputs["L1_ffn_A"])
    d["A1catT"] = np.ascontiguousarray(A1.reshape(128, D).T)
    B1 = np.asarray(inputs["L1_ffn_Bw"])
    d["B1flatT"] = np.ascontiguousarray(B1.transpose(0, 2, 1).reshape(128, D))
    d["eye_f32"] = np.eye(128, dtype=np.float32)
    d["ind8"] = np.kron(np.eye(8, dtype=np.float32),
                        np.ones((1, 16), np.float32))
    d["qmask"] = []
    for c in range(NCORES):
        m = np.zeros((TL,), np.float32)
        lo = max(Q_START - c * TL, 0)
        hi = min(Q_END - c * TL, TL)
        if hi > lo:
            m[lo:hi] = 1.0
        d["qmask"].append(np.ascontiguousarray(m.reshape(8, 128).T))
    return d


def build_launch_A():
    nc = bass.Bass("TRN2", target_bir_lowering=False, debug=False)
    x_sl = nc.dram_tensor("x_sl", [TL, D], F32, kind="ExternalInput").ap()
    xpwT = nc.dram_tensor("xpwT", [D, 160], F32, kind="ExternalInput").ap()
    eyeb = nc.dram_tensor("eye_f32", [128, 128], F32, kind="ExternalInput").ap()
    xpT_out = nc.dram_tensor("xpT_out", [160, TL], F32, kind="ExternalOutput").ap()

    stack = ExitStack()
    with stack:
        sb = lambda name, shape, dt=F32: stack.enter_context(
            nc.sbuf_tensor(name, shape, dt)).ap()
        psa = lambda name, shape, dt=F32: stack.enter_context(
            nc.psum_tensor(name, shape, dt)).ap()

        xt = sb("xt", [128, 2, D], BF16)
        x0T = sb("x0T", [128, 32 * TL], BF16)
        eye = sb("eye", [128, 128], BF16)
        xpw = sb("xpw", [128, 32, 160], BF16)
        xpT = sb("xpT", [128, TL], F32)
        xpT2 = sb("xpT2", [32, TL], F32)
        psT = psa("psT", [128, 2, 1024], BF16)
        psA = psa("psA", [128, 4, 512], F32)

        p = Prog(nc, stack)
        ld = p.sem("ld")
        xl0 = p.sem("xl0")
        xl1 = p.sem("xl1")
        tp = p.sem("tp")
        ev = p.sem("ev")
        mm = p.sem("mm")
        ev2 = p.sem("ev2")
        st = p.sem("st")

        def ph0_g(g):
            g.dma_start(out=eye[:], in_=eyeb).then_inc(ld, 16)
            g.dma_start(out=xpw[:],
                        in_=xpwT.rearrange("(c p) r -> p c r", p=128)
                        ).then_inc(ld, 16)
            g.wait_ge(ld, 32)
            return None
        p.phase(gpsimd=ph0_g)

        def ph1_g(g):
            for tt in range(8):
                if tt >= 2:
                    g.wait_ge(ev, (tt - 1) * 32)
                g.dma_start(out=xt[:, tt % 2],
                            in_=x_sl[tt * 128:(tt + 1) * 128, :]
                            ).then_inc(xl0 if tt % 2 == 0 else xl1, 16)
            g.wait_ge(xl0, 64)
            g.wait_ge(xl1, 64)
            return None

        def ph1_t(t):
            last = None
            for tt in range(8):
                t.wait_ge(xl0 if tt % 2 == 0 else xl1, (tt // 2 + 1) * 16)
                for dc in range(32):
                    k = tt * 32 + dc
                    if k >= 2:
                        t.wait_ge(ev, k - 1)
                    t.transpose(psT[:, k % 2, 0:128],
                                xt[:, tt % 2, dc * 128:(dc + 1) * 128],
                                eye[:]).then_inc(tp, 1)
            t.wait_ge(tp, 256)
            return None

        def ph1_v(v):
            last = None
            for tt in range(8):
                for dc in range(32):
                    k = tt * 32 + dc
                    v.wait_ge(tp, k + 1)
                    v.tensor_copy(
                        x0T[:, dc * TL + tt * 128: dc * TL + tt * 128 + 128],
                        psT[:, k % 2, 0:128]).then_inc(ev, 1)
            v.wait_ge(ev, 256)
            return None
        p.phase(gpsimd=ph1_g, tensor=ph1_t, vector=ph1_v)

        def ph2_t(t):
            last = None
            for tc in range(2):
                for rc in range(2):
                    j = tc * 2 + rc
                    pslice = psA[:, j]
                    w = 128 if rc == 0 else 32
                    for dc in range(32):
                        last = t.matmul(
                            pslice[0:w, :],
                            xpw[:, dc, rc * 128: rc * 128 + w],
                            x0T[:, dc * TL + tc * 512: dc * TL + tc * 512 + 512],
                            start=(dc == 0), stop=(dc == 31))
                    last.then_inc(mm, 1)
            t.wait_ge(mm, 4)
            return None

        def ph2_v(v):
            last = None
            for tc in range(2):
                for rc in range(2):
                    j = tc * 2 + rc
                    v.wait_ge(mm, j + 1)
                    w = 128 if rc == 0 else 32
                    dst = xpT if rc == 0 else xpT2
                    v.tensor_copy(
                        dst[0:w, tc * 512:(tc + 1) * 512],
                        psA[:, j][0:w, :]).then_inc(ev2, 1)
            v.wait_ge(ev2, 4)
            return None
        p.phase(tensor=ph2_t, vector=ph2_v)

        def ph3_s(sy):
            sy.wait_ge(ev2, 4)
            sy.dma_start(out=xpT_out[0:128, :], in_=xpT[:]).then_inc(st, 16)
            sy.dma_start(out=xpT_out[128:160, :], in_=xpT2[:]).then_inc(st, 16)
            sy.wait_ge(st, 32)
            return None
        p.phase(sync=ph3_s)

        p.finalize()
    return nc


def build_launch_B():
    nc = bass.Bass("TRN2", target_bir_lowering=False, debug=False)
    x_cols = nc.dram_tensor("x_cols", [T, DSL], F32, kind="ExternalInput").ap()
    xpT_full = nc.dram_tensor("xpT_full", [160, T], F32, kind="ExternalInput").ap()
    Alog = nc.dram_tensor("Alog_sl", [128, 4 * NST], F32, kind="ExternalInput").ap()
    Dsl_i = nc.dram_tensor("D_sl", [128, 4], F32, kind="ExternalInput").ap()
    dtpwT = nc.dram_tensor("dtpwT_sl", [RK, DSL], F32, kind="ExternalInput").ap()
    dtpb = nc.dram_tensor("dtpb_sl", [128, 4], F32, kind="ExternalInput").ap()
    eyef = nc.dram_tensor("eye_f32", [128, 128], F32, kind="ExternalInput").ap()
    ymixT = nc.dram_tensor("ymixT", [DSL, T], F32, kind="ExternalOutput").ap()

    TH = 2048
    NTC = TH // 512
    NXT = TH // 128

    stack = ExitStack()
    with stack:
        sb = lambda name, shape, dt=F32: stack.enter_context(
            nc.sbuf_tensor(name, shape, dt)).ap()
        psa = lambda name, shape, dt=F32: stack.enter_context(
            nc.psum_tensor(name, shape, dt)).ap()

        negA = sb("negA", [128, 4 * NST])
        Dcol = sb("Dcol", [128, 4])
        dtb = sb("dtb", [128, 4])
        dtw = sb("dtw", [128, 4, 128], BF16)
        eye = sb("eye", [128, 128], F32)
        dtin = sb("dtin", [128, 2, 512], BF16)
        dt_t = sb("dt_t", [128, TH])
        sp1 = sb("sp1", [128, 2, 512])
        sp2 = sb("sp2", [128, 2, 512])
        xTs = sb("xTs", [128, 2, 128])
        xT = sb("xT", [128, TH])
        dtx = sb("dtx", [128, TH])
        dAb = sb("dAb", [128, TH])
        dBxb = sb("dBxb", [128, 2, TH])
        hbuf = sb("hbuf", [128, TH])
        hcb = sb("hcb", [128, 2, TH])
        Bb = sb("Bb", [128, 2, TH])
        Cb = sb("Cb", [128, 2, TH])
        yac = sb("yac", [128, TH])
        states = sb("states", [128, NST])
        psA = psa("psA", [128, 4, 512], F32)

        p = Prog(nc, stack)
        ld = p.sem("ld")
        s_ex = p.sem("s_ex")
        s_ng = p.sem("s_ng")

        def ph0_g(g):
            g.dma_start(out=negA[:], in_=Alog).then_inc(ld, 16)
            g.dma_start(out=Dcol[:], in_=Dsl_i).then_inc(ld, 16)
            g.dma_start(out=dtb[:], in_=dtpb).then_inc(ld, 16)
            g.dma_start(out=dtw[:],
                        in_=dtpwT.rearrange("r (c p) -> r c p", p=128)
                        ).then_inc(ld, 16)
            g.dma_start(out=eye[:], in_=eyef).then_inc(ld, 16)
            g.wait_ge(ld, 80)
            return None

        def ph0_s(s):
            s.wait_ge(ld, 80)
            s.activation(negA[:], negA[:], AF.Exp, bias=0.0,
                         scale=1.0).then_inc(s_ex, 1)
            s.wait_ge(s_ex, 1)
            return None

        def ph0_v(v):
            v.wait_ge(s_ex, 1)
            v.tensor_scalar(negA[:], negA[:], -1.0, None,
                            ALU.mult).then_inc(s_ng, 1)
            v.wait_ge(s_ng, 1)
            return None
        p.phase(gpsimd=ph0_g, scalar=ph0_s, vector=ph0_v)

        S = {n: p.sem(n) for n in
             ("di0", "di1", "md", "sp", "sv", "xl0", "xl1", "xt", "xe", "dx",
              "bB0", "bB1", "bC0", "bC1", "da",
              "db", "sc", "cp", "hc", "ya", "fi", "so")}
        PER = {"di0": 16 * (NTC // 2), "di1": 16 * (NTC // 2), "md": NTC,
               "sp": 4 * NTC, "sv": NTC,
               "xl0": 16 * (NXT // 2), "xl1": 16 * (NXT // 2),
               "xt": NXT, "xe": NXT, "dx": 1,
               "bB0": 16 * 8, "bB1": 16 * 8, "bC0": 16 * 8, "bC1": 16 * 8,
               "da": 16, "db": 16, "sc": 16, "hc": 16, "ya": 15, "fi": 1,
               "so": 16}

        cp_base = 0
        for dc in range(4):
            for th in range(4):
                pi = dc * 4 + th
                B = {k: PER[k] * pi for k in PER}
                B["cp"] = cp_base
                n_cp = 16 if th < 3 else 0
                cp_base += n_cp

                def ph_g(g, dc=dc, th=th, B=B):
                    for tc in range(NTC):
                        if tc >= 2:
                            g.wait_ge(S["md"], B["md"] + tc - 1)
                        g.dma_start(
                            out=dtin[:, tc % 2],
                            in_=xpT_full[0:128,
                                         th * TH + tc * 512: th * TH + tc * 512 + 512]
                        ).then_inc(S["di0" if tc % 2 == 0 else "di1"], 16)
                    for n in range(16):
                        g.wait_ge(S["bB0" if n % 2 == 0 else "bB1"],
                                  B["bB0"] + (n // 2 + 1) * 16)
                        if n >= 2:
                            g.wait_ge(S["sc"], B["sc"] + n - 1)
                        if n == 0:
                            g.wait_ge(S["dx"], B["dx"] + 1)
                        g.tensor_tensor(dBxb[:, n % 2], dtx[:], Bb[:, n % 2],
                                        ALU.mult).then_inc(S["db"], 1)
                        if n >= 2:
                            k = n - 1
                            g.wait_ge(S["hc"], B["hc"] + k + 1)
                            if k >= 2:
                                g.wait_ge(S["ya"], B["ya"] + k - 1)
                            g.tensor_tensor(yac[:], yac[:], hcb[:, k % 2],
                                            ALU.add).then_inc(S["ya"], 1)
                    g.wait_ge(S["hc"], B["hc"] + 16)
                    g.wait_ge(S["ya"], B["ya"] + 14)
                    g.tensor_tensor(yac[:], yac[:], hcb[:, 15 % 2],
                                    ALU.add).then_inc(S["ya"], 1)
                    g.wait_ge(S["ya"], B["ya"] + 15)
                    return None

                def ph_t(t, dc=dc, th=th, B=B):
                    for tc in range(NTC):
                        t.wait_ge(S["di0" if tc % 2 == 0 else "di1"],
                                  B["di0"] + (tc // 2 + 1) * 16)
                        if tc >= 2:
                            t.wait_ge(S["sp"], B["sp"] + 4 * (tc - 2) + 2)
                        t.matmul(psA[:, tc % 2], dtw[:, dc],
                                 dtin[:, tc % 2], start=True,
                                 stop=True).then_inc(S["md"], 1)
                    for i in range(NXT):
                        t.wait_ge(S["xl0" if i % 2 == 0 else "xl1"],
                                  B["xl0"] + (i // 2 + 1) * 16)
                        if i >= 2:
                            t.wait_ge(S["xe"], B["xe"] + i - 1)
                        t.transpose(psA[:, 2 + i % 2][:, 0:128], xTs[:, i % 2],
                                    eye[:]).then_inc(S["xt"], 1)
                    t.wait_ge(S["xt"], B["xt"] + NXT)
                    return None

                def ph_sy(sy, dc=dc, th=th, B=B):
                    for i in range(NXT):
                        if i >= 2:
                            sy.wait_ge(S["xt"], B["xt"] + i - 1)
                        r0 = th * TH + i * 128
                        sy.dma_start(out=xTs[:, i % 2],
                                     in_=x_cols[r0:r0 + 128,
                                                dc * 128:(dc + 1) * 128]
                                     ).then_inc(S["xl0" if i % 2 == 0 else "xl1"], 16)
                    for n in range(16):
                        if n >= 2:
                            sy.wait_ge(S["db"], B["db"] + n - 1)
                            sy.wait_ge(S["hc"], B["hc"] + n - 1)
                        sy.dma_start(
                            out=Bb[:, n % 2],
                            in_=xpT_full[128 + n: 129 + n, th * TH: (th + 1) * TH]
                            .partition_broadcast(128).opt()
                            ).then_inc(S["bB0" if n % 2 == 0 else "bB1"], 16)
                        sy.dma_start(
                            out=Cb[:, n % 2],
                            in_=xpT_full[144 + n: 145 + n, th * TH: (th + 1) * TH]
                            .partition_broadcast(128).opt()
                            ).then_inc(S["bC0" if n % 2 == 0 else "bC1"], 16)
                    sy.wait_ge(S["fi"], B["fi"] + 1)
                    sy.dma_start(out=ymixT[dc * 128:(dc + 1) * 128,
                                           th * TH:(th + 1) * TH],
                                 in_=dAb[:]).then_inc(S["so"], 16)
                    sy.wait_ge(S["so"], B["so"] + 16)
                    return None

                def ph_sc(s, dc=dc, th=th, B=B):
                    bcol = dtb[:, dc:dc + 1]
                    for tc in range(NTC):
                        s.wait_ge(S["md"], B["md"] + tc + 1)
                        if tc >= 2:
                            s.wait_ge(S["sv"], B["sv"] + tc - 1)
                        u = psA[:, tc % 2]
                        s.activation(sp1[:, tc % 2], u, AF.Relu, bias=bcol,
                                     scale=1.0).then_inc(S["sp"], 1)
                        s.activation(sp2[:, tc % 2], u, AF.Abs, bias=bcol,
                                     scale=1.0).then_inc(S["sp"], 1)
                        s.wait_ge(S["sp"], B["sp"] + 4 * tc + 2)
                        s.activation(sp2[:, tc % 2], sp2[:, tc % 2], AF.Exp,
                                     bias=0.0, scale=-1.0).then_inc(S["sp"], 1)
                        s.wait_ge(S["sp"], B["sp"] + 4 * tc + 3)
                        s.activation(sp2[:, tc % 2], sp2[:, tc % 2], AF.Ln,
                                     bias=1.0, scale=1.0).then_inc(S["sp"], 1)
                    for n in range(16):
                        s.wait_ge(S["sv"], B["sv"] + NTC)
                        if n >= 1:
                            s.wait_ge(S["sc"], B["sc"] + n)
                        acol = negA[:, dc * NST + n: dc * NST + n + 1]
                        s.activation(dAb[:], dt_t[:], AF.Exp, bias=0.0,
                                     scale=acol).then_inc(S["da"], 1)
                    s.wait_ge(S["da"], B["da"] + 16)
                    return None

                def ph_v(v, dc=dc, th=th, B=B, n_cp=n_cp):
                    for tc in range(NTC):
                        v.wait_ge(S["sp"], B["sp"] + 4 * tc + 4)
                        v.tensor_tensor(dt_t[:, tc * 512:(tc + 1) * 512],
                                        sp1[:, tc % 2], sp2[:, tc % 2],
                                        ALU.add).then_inc(S["sv"], 1)
                    for i in range(NXT):
                        v.wait_ge(S["xt"], B["xt"] + i + 1)
                        v.tensor_copy(xT[:, i * 128:(i + 1) * 128],
                                      psA[:, 2 + i % 2][:, 0:128]).then_inc(S["xe"], 1)
                    v.wait_ge(S["sv"], B["sv"] + NTC)
                    v.wait_ge(S["xe"], B["xe"] + NXT)
                    v.tensor_tensor(dtx[:], dt_t[:], xT[:], ALU.mult
                                    ).then_inc(S["dx"], 1)
                    for n in range(16):
                        v.wait_ge(S["da"], B["da"] + n + 1)
                        v.wait_ge(S["db"], B["db"] + n + 1)
                        if n >= 1:
                            v.wait_ge(S["hc"], B["hc"] + n)
                        init = 0.0 if th == 0 else states[:, n:n + 1]
                        v.tensor_tensor_scan(hbuf[:], dAb[:], dBxb[:, n % 2],
                                             init, ALU.mult, ALU.add
                                             ).then_inc(S["sc"], 1)
                        if th < 3:
                            v.wait_ge(S["sc"], B["sc"] + n + 1)
                            v.tensor_copy(states[:, n:n + 1],
                                          hbuf[:, TH - 1:TH]).then_inc(S["cp"], 1)
                        v.wait_ge(S["sc"], B["sc"] + n + 1)
                        v.wait_ge(S["bC0" if n % 2 == 0 else "bC1"],
                                  B["bC0"] + (n // 2 + 1) * 16)
                        if n == 0:
                            v.tensor_tensor(yac[:], hbuf[:], Cb[:, 0],
                                            ALU.mult).then_inc(S["hc"], 1)
                        else:
                            if n >= 3:
                                v.wait_ge(S["ya"], B["ya"] + n - 2)
                            v.tensor_tensor(hcb[:, n % 2], hbuf[:], Cb[:, n % 2],
                                            ALU.mult).then_inc(S["hc"], 1)
                    v.wait_ge(S["ya"], B["ya"] + 15)
                    v.scalar_tensor_tensor(dAb[:], xT[:],
                                           Dcol[:, dc:dc + 1], yac[:],
                                           ALU.mult, ALU.add).then_inc(S["fi"], 1)
                    v.wait_ge(S["fi"], B["fi"] + 1)
                    return None

                p.phase(gpsimd=ph_g, tensor=ph_t, sync=ph_sy, scalar=ph_sc,
                        vector=ph_v)

        p.finalize()
    return nc



def build_launch_C():
    nc = bass.Bass("TRN2", target_bir_lowering=False, debug=False)
    x_sl = nc.dram_tensor("x_sl", [TL, D], F32, kind="ExternalInput").ap()
    ymix_sl = nc.dram_tensor("ymix_sl", [TL, D], F32, kind="ExternalInput").ap()
    qmask = nc.dram_tensor("qmask", [128, 8], F32, kind="ExternalInput").ap()
    x1_out = nc.dram_tensor("x1_out", [TL, D], F32, kind="ExternalOutput").ap()
    qp_out = nc.dram_tensor("qp_out", [1, D], F32, kind="ExternalOutput").ap()

    stack = ExitStack()
    with stack:
        sb = lambda name, shape, dt=F32: stack.enter_context(
            nc.sbuf_tensor(name, shape, dt)).ap()
        psa = lambda name, shape, dt=F32: stack.enter_context(
            nc.psum_tensor(name, shape, dt)).ap()

        xt = sb("xt", [128, 2, D])
        yt = sb("yt", [128, 2, D])
        zt = sb("zt", [128, 2, D])
        x1t = sb("x1t", [128, 2, D])
        x1b = sb("x1b", [128, 2, D], BF16)
        sq = sb("sq", [128, 2, D], BF16)
        qmb = sb("qmb", [128, 8], BF16)
        qp = sb("qp", [1, D])
        srow = sb("srow", [128, 2, 8])
        PS = psa("PS", [128, 8, 512], F32)

        p = Prog(nc, stack)
        ld = p.sem("ld")
        xl0, xl1 = p.sem("xl0"), p.sem("xl1")
        yl0, yl1 = p.sem("yl0"), p.sem("yl1")
        s_z = p.sem("s_z")
        s_sq = p.sem("s_sq")
        s_st = p.sem("s_st")
        s_rs = p.sem("s_rs")
        s_ap = p.sem("s_ap")
        s_bf = p.sem("s_bf")
        s_mm = p.sem("s_mm")
        s_ev = p.sem("s_ev")
        so0 = p.sem("so0")
        so1 = p.sem("so1")
        s_qs = p.sem("s_qs")

        def ph0_g(g):
            g.dma_start(out=qmb[:], in_=qmask).then_inc(ld, 16)
            g.wait_ge(ld, 16)
            return None
        p.phase(gpsimd=ph0_g)

        def ph1_sy2(sy):
            for tt in range(8):
                if tt >= 2:
                    sy.wait_ge(s_bf, tt - 1)
                sy.dma_start(out=xt[:, tt % 2],
                             in_=x_sl[tt * 128:(tt + 1) * 128, :]
                             ).then_inc(xl0 if tt % 2 == 0 else xl1, 16)
                sy.dma_start(out=yt[:, tt % 2],
                             in_=ymix_sl[tt * 128:(tt + 1) * 128, :]
                             ).then_inc(yl0 if tt % 2 == 0 else yl1, 16)
                if tt >= 1:
                    sy.wait_ge(s_ap, tt)
                    k = tt - 1
                    sy.dma_start(out=x1_out[k * 128: (k + 1) * 128, :],
                                 in_=x1t[:, k % 2]
                                 ).then_inc(so0 if k % 2 == 0 else so1, 16)
            sy.wait_ge(s_ap, 8)
            sy.dma_start(out=x1_out[7 * 128: 8 * 128, :],
                         in_=x1t[:, 7 % 2]).then_inc(so1, 16)
            sy.wait_ge(s_ev, 8)
            sy.dma_start(out=qp_out, in_=qp[:]).then_inc(s_qs, 16)
            sy.wait_ge(so0, 64)
            sy.wait_ge(so1, 64)
            sy.wait_ge(s_qs, 16)
            return None

        def ph1_v(v):
            for tt in range(8):
                v.wait_ge(xl0 if tt % 2 == 0 else xl1, (tt // 2 + 1) * 16)
                v.wait_ge(yl0 if tt % 2 == 0 else yl1, (tt // 2 + 1) * 16)
                if tt >= 2:
                    v.wait_ge(so0 if tt % 2 == 0 else so1, 16 * (tt // 2))
                    v.wait_ge(s_sq, tt - 1)
                    v.wait_ge(s_mm, 8 * (tt - 1))
                v.tensor_tensor(zt[:, tt % 2], xt[:, tt % 2], yt[:, tt % 2],
                                ALU.add).then_inc(s_z, 1)
                v.wait_ge(s_z, 4 * tt + 1)
                v.tensor_reduce(srow[:, tt % 2, 0:1], zt[:, tt % 2], AX,
                                ALU.add).then_inc(s_st, 1)
                v.wait_ge(s_sq, tt + 1)
                v.wait_ge(s_st, 3 * tt + 1)
                v.tensor_scalar(srow[:, tt % 2, 2:3], srow[:, tt % 2, 0:1],
                                1.0 / D, None, ALU.mult).then_inc(s_st, 1)
                v.wait_ge(s_st, 3 * tt + 2)
                v.tensor_tensor(srow[:, tt % 2, 3:4], srow[:, tt % 2, 2:3],
                                srow[:, tt % 2, 2:3], ALU.mult).then_inc(s_st, 1)
                v.wait_ge(s_st, 3 * tt + 3)
                v.scalar_tensor_tensor(srow[:, tt % 2, 4:5],
                                       srow[:, tt % 2, 1:2], 1.0 / D,
                                       srow[:, tt % 2, 3:4],
                                       ALU.mult, ALU.subtract).then_inc(s_z, 1)
                v.wait_ge(s_z, 4 * tt + 2)
                v.tensor_scalar(srow[:, tt % 2, 7:8], srow[:, tt % 2, 4:5],
                                EPS, None, ALU.add).then_inc(s_z, 1)
                v.wait_ge(s_rs, tt + 1)
                v.reciprocal(srow[:, tt % 2, 6:7],
                             srow[:, tt % 2, 5:6]).then_inc(s_z, 1)
                v.wait_ge(s_z, 4 * tt + 4)
                v.tensor_scalar(x1t[:, tt % 2], zt[:, tt % 2],
                                srow[:, tt % 2, 2:3], srow[:, tt % 2, 6:7],
                                ALU.subtract, ALU.mult).then_inc(s_ap, 1)
                v.wait_ge(s_ap, tt + 1)
                v.tensor_copy(x1b[:, tt % 2], x1t[:, tt % 2]).then_inc(s_bf, 1)
            v.wait_ge(s_mm, 64)
            for c in range(8):
                v.tensor_copy(qp[0:1, c * 512:(c + 1) * 512],
                              PS[:, c][0:1, :]).then_inc(s_ev, 1)
            v.wait_ge(s_ev, 8)
            return None


        def ph1_sc(s):
            for tt in range(8):
                s.wait_ge(s_z, 4 * tt + 1)
                s.activation(sq[:, tt % 2], zt[:, tt % 2], AF.Square,
                             bias=0.0, scale=1.0,
                             accum_out=srow[:, tt % 2, 1:2]).then_inc(s_sq, 1)
                s.wait_ge(s_z, 4 * tt + 3)
                s.activation(srow[:, tt % 2, 5:6], srow[:, tt % 2, 7:8],
                             AF.Sqrt, bias=0.0, scale=1.0).then_inc(s_rs, 1)
            s.wait_ge(s_rs, 8)
            return None

        def ph1_t(t):
            for tt in range(8):
                t.wait_ge(s_bf, tt + 1)
                for c in range(8):
                    t.matmul(PS[:, c][0:1, :], qmb[:, tt:tt + 1],
                             x1b[:, tt % 2, c * 512:(c + 1) * 512],
                             start=(tt == 0), stop=(tt == 7)).then_inc(s_mm, 1)
            t.wait_ge(s_mm, 64)
            return None

        p.phase(sync=ph1_sy2, vector=ph1_v, scalar=ph1_sc, tensor=ph1_t)
        p.finalize()
    return nc


def add_ln_phase(p, nc, stack, src_dram, dst_dram, pref, zt=None, bufs=None):
    sb = lambda name, shape, dt=F32: stack.enter_context(
        nc.sbuf_tensor(name, shape, dt)).ap()
    if zt is None:
        zt = sb(pref + "zt", [128, 2, D])
    if bufs is None:
        ot = sb(pref + "ot", [128, 1, D])
        sq = sb(pref + "sq", [128, 2, D], BF16)
        srow = sb(pref + "srow", [128, 2, 8])
        bufs = (ot, sq, srow)
    ot, sq, srow = bufs
    nz = zt.shape[1]
    zl0, zl1 = p.sem(pref + "zl0"), p.sem(pref + "zl1")
    s_z = p.sem(pref + "sz")
    s_st = p.sem(pref + "sst")
    s_sq = p.sem(pref + "ssq")
    s_rs = p.sem(pref + "srs")
    s_ap = p.sem(pref + "sap")
    so0, so1 = p.sem(pref + "so0"), p.sem(pref + "so1")

    def ph_sy(sy):
        for tt in range(8):
            if tt >= nz:
                sy.wait_ge(s_ap, tt - nz + 1)
                sy.wait_ge(s_sq, tt - nz + 1)
            sy.dma_start(out=zt[:, tt % nz],
                         in_=src_dram[tt * 128:(tt + 1) * 128, :]
                         ).then_inc(zl0 if tt % 2 == 0 else zl1, 16)
            if tt >= 1:
                k = tt - 1
                sy.wait_ge(s_ap, tt)
                sy.dma_start(out=dst_dram[k * 128:(k + 1) * 128, :],
                             in_=ot[:, 0]
                             ).then_inc(so0 if k % 2 == 0 else so1, 16)
        sy.wait_ge(s_ap, 8)
        sy.dma_start(out=dst_dram[7 * 128:8 * 128, :],
                     in_=ot[:, 0]).then_inc(so1, 16)
        sy.wait_ge(so0, 64)
        sy.wait_ge(so1, 64)
        return None

    def ph_v(v):
        for tt in range(8):
            v.wait_ge(zl0 if tt % 2 == 0 else zl1, (tt // 2 + 1) * 16)
            if tt >= 2:
                v.wait_ge(so0 if tt % 2 == 0 else so1, 16 * (tt // 2))
                v.wait_ge(s_sq, tt - 1)
            v.tensor_reduce(srow[:, tt % 2, 0:1], zt[:, tt % nz], AX,
                            ALU.add).then_inc(s_st, 1)
            v.wait_ge(s_sq, tt + 1)
            v.wait_ge(s_st, 3 * tt + 1)
            v.tensor_scalar(srow[:, tt % 2, 2:3], srow[:, tt % 2, 0:1],
                            1.0 / D, None, ALU.mult).then_inc(s_st, 1)
            v.wait_ge(s_st, 3 * tt + 2)
            v.tensor_tensor(srow[:, tt % 2, 3:4], srow[:, tt % 2, 2:3],
                            srow[:, tt % 2, 2:3], ALU.mult).then_inc(s_st, 1)
            v.wait_ge(s_st, 3 * tt + 3)
            v.scalar_tensor_tensor(srow[:, tt % 2, 4:5], srow[:, tt % 2, 1:2],
                                   1.0 / D, srow[:, tt % 2, 3:4],
                                   ALU.mult, ALU.subtract).then_inc(s_z, 1)
            v.wait_ge(s_z, 3 * tt + 1)
            v.tensor_scalar(srow[:, tt % 2, 7:8], srow[:, tt % 2, 4:5],
                            EPS, None, ALU.add).then_inc(s_z, 1)
            v.wait_ge(s_rs, tt + 1)
            v.reciprocal(srow[:, tt % 2, 6:7],
                         srow[:, tt % 2, 5:6]).then_inc(s_z, 1)
            v.wait_ge(s_z, 3 * tt + 3)
            if tt >= 1:
                k = tt - 1
                v.wait_ge(so0 if k % 2 == 0 else so1, 16 * (k // 2 + 1))
            v.tensor_scalar(ot[:, 0], zt[:, tt % nz],
                            srow[:, tt % 2, 2:3], srow[:, tt % 2, 6:7],
                            ALU.subtract, ALU.mult).then_inc(s_ap, 1)
        v.wait_ge(s_ap, 8)
        return None

    def ph_sc(s):
        for tt in range(8):
            s.wait_ge(zl0 if tt % 2 == 0 else zl1, (tt // 2 + 1) * 16)
            if tt >= 2:
                s.wait_ge(s_ap, tt - 1)
            s.activation(sq[:, tt % 2], zt[:, tt % nz], AF.Square,
                         bias=0.0, scale=1.0,
                         accum_out=srow[:, tt % 2, 1:2]).then_inc(s_sq, 1)
            s.wait_ge(s_z, 3 * tt + 2)
            s.activation(srow[:, tt % 2, 5:6], srow[:, tt % 2, 7:8],
                         AF.Sqrt, bias=0.0, scale=1.0).then_inc(s_rs, 1)
        s.wait_ge(s_rs, 8)
        return None

    p.phase(sync=ph_sy, vector=ph_v, scalar=ph_sc)
    return bufs


def build_launch_D():
    nc = bass.Bass("TRN2", target_bir_lowering=False, debug=False)
    x1_sl = nc.dram_tensor("x1_sl", [TL, D], F32, kind="ExternalInput").ap()
    qparts = nc.dram_tensor("qparts", [8, D], F32, kind="ExternalInput").ap()
    W0T = nc.dram_tensor("W0T", [D, D], F32, kind="ExternalInput").ap()
    b0 = nc.dram_tensor("b0", [1, D], F32, kind="ExternalInput").ap()
    A0catT = nc.dram_tensor("A0catT", [D, 128], F32, kind="ExternalInput").ap()
    B0flatT = nc.dram_tensor("B0flatT", [128, D], F32, kind="ExternalInput").ap()
    rWT = nc.dram_tensor("rWT", [D, 8], F32, kind="ExternalInput").ap()
    rb = nc.dram_tensor("rb", [8, 1], F32, kind="ExternalInput").ap()
    ind8 = nc.dram_tensor("ind8", [8, 128], F32, kind="ExternalInput").ap()
    eyef = nc.dram_tensor("eye_f32", [128, 128], F32, kind="ExternalInput").ap()
    x2_out = nc.dram_tensor("x2_out", [TL, D], F32, kind="ExternalOutput").ap()
    z_dram = nc.dram_tensor("z_dram", [TL, D], F32).ap()

    stack = ExitStack()
    with stack:
        sb = lambda name, shape, dt=F32: stack.enter_context(
            nc.sbuf_tensor(name, shape, dt)).ap()
        psa = lambda name, shape, dt=F32: stack.enter_context(
            nc.psum_tensor(name, shape, dt)).ap()

        PS = psa("PS", [128, 8, 512], F32)
        x1t = sb("x1t", [128, 2, D])
        x1T = sb("x1T", [128, 32, TL], BF16)
        eye = sb("eye", [128, 128], F32)
        qpb = sb("qpb", [8, D], BF16)
        ones8 = sb("ones8", [8, 1], BF16)
        one1 = sb("one1", [1, 128], BF16)
        rW_b = sb("rW_b", [128, 32, 8], BF16)
        rbf = sb("rbf", [8, 1])
        ind8b = sb("ind8b", [8, 128], BF16)
        A0c = sb("A0c", [128, 32, 128], BF16)
        B0f = sb("B0f", [128, D], BF16)
        b0b = sb("b0b", [1, D], BF16)
        qrow = sb("qrow", [1, D])
        eyeb = sb("eyeb", [128, 128], BF16)
        qT_b = sb("qT_b", [128, 32], BF16)
        lrow = sb("lrow", [1, 16])
        lg8 = sb("lg8", [8, 2])
        r8b = sb("r8b", [8, 1], BF16)
        rer = sb("rer", [128, 1])
        tpT = sb("tpT", [128, TL], BF16)
        wch = sb("wch", [128, 2, 512], BF16)
        x1c = sb("x1c", [128, 2, 512])
        zc = sb("zc", [128, 2, 512])

        p = Prog(nc, stack)
        ld = p.sem("ld")
        ld2 = p.sem("ld2")
        s_ms = p.sem("s_ms")
        xl0, xl1 = p.sem("xl0"), p.sem("xl1")
        s_tp = p.sem("s_tp")
        s_xe = p.sem("s_xe")
        s_r = p.sem("s_r")
        s_tm = p.sem("s_tm")
        s_te = p.sem("s_te")
        wl0, wl1 = p.sem("wl0"), p.sem("wl1")
        s_bm = p.sem("s_bm")
        rl0, rl1 = p.sem("rl0"), p.sem("rl1")
        s_ze = p.sem("s_ze")
        zs0, zs1 = p.sem("zs0"), p.sem("zs1")

        def ph0_g(g):
            g.dma_start(out=rW_b[:], in_=rWT.rearrange("(c p) r -> p c r", p=128)
                        ).then_inc(ld, 16)
            g.dma_start(out=ind8b[:], in_=ind8).then_inc(ld, 16)
            g.dma_start(out=A0c[:], in_=A0catT.rearrange("(c p) r -> p c r", p=128)
                        ).then_inc(ld, 16)
            g.dma_start(out=B0f[:], in_=B0flatT).then_inc(ld, 16)
            g.dma_start(out=b0b[:], in_=b0).then_inc(ld, 16)
            g.dma_start(out=qpb[:], in_=qparts).then_inc(ld, 16)
            g.memset(ones8[:], 1.0).then_inc(s_ms, 1)
            g.memset(one1[:], 1.0).then_inc(s_ms, 1)
            g.wait_ge(ld, 96)
            g.wait_ge(s_ms, 2)
            return None

        def ph0_sy(sy):
            sy.dma_start(out=rbf[:], in_=rb).then_inc(ld2, 16)
            sy.dma_start(out=eye[:], in_=eyef).then_inc(ld2, 16)
            sy.wait_ge(ld2, 32)
            return None

        p.phase(gpsimd=ph0_g, sync=ph0_sy)

        def ph1_sy(sy):
            for tt in range(8):
                if tt >= 2:
                    sy.wait_ge(s_tp, (tt - 1) * 32)
                sy.dma_start(out=x1t[:, tt % 2],
                             in_=x1_sl[tt * 128:(tt + 1) * 128, :]
                             ).then_inc(xl0 if tt % 2 == 0 else xl1, 16)
            return None

        def ph1_t(t):
            for tt in range(8):
                t.wait_ge(xl0 if tt % 2 == 0 else xl1, (tt // 2 + 1) * 16)
                for dc in range(32):
                    k = tt * 32 + dc
                    if k >= 2:
                        t.wait_ge(s_xe, k - 1)
                    t.transpose(PS[:, k % 2][:, 0:128],
                                x1t[:, tt % 2, dc * 128:(dc + 1) * 128],
                                eye[:]).then_inc(s_tp, 1)
            t.wait_ge(s_tp, 256)
            return None

        def ph1_v(v):
            for tt in range(8):
                for dc in range(32):
                    k = tt * 32 + dc
                    v.wait_ge(s_tp, k + 1)
                    v.tensor_copy(
                        x1T[:, dc, tt * 128:(tt + 1) * 128],
                        PS[:, k % 2][:, 0:128]).then_inc(s_xe, 1)
            v.wait_ge(s_xe, 256)
            return None
        p.phase(sync=ph1_sy, tensor=ph1_t, vector=ph1_v)

        def ph2_v(v):
            v.tensor_copy(eyeb[:], eye[:]).then_inc(s_r, 1)
            for c in range(8):
                v.wait_ge(s_r, 2 + 2 * c)
                v.tensor_scalar(qrow[0:1, c * 512:(c + 1) * 512],
                                PS[:, 4 + c % 2][0:1, :], 1.0 / NQTOK, None,
                                ALU.mult).then_inc(s_r, 1)
            for c in range(32):
                v.wait_ge(s_r, 18 + 2 * c)
                v.tensor_copy(qT_b[:, c:c + 1],
                              PS[:, 6 + c % 2][:, 0:1]).then_inc(s_r, 1)
            v.wait_ge(s_r, 82)
            v.tensor_tensor(lg8[:, 0:1], PS[:, 4][0:8, 0:1], rbf[:],
                            ALU.add).then_inc(s_r, 1)
            v.wait_ge(s_r, 84)
            v.tensor_copy(lrow[0:1, 0:8], PS[:, 5][0:1, 0:8]).then_inc(s_r, 1)
            v.wait_ge(s_r, 85)
            v.reduce_max(lrow[0:1, 8:9], lrow[0:1, 0:8],
                         AX).then_inc(s_r, 1)
            v.wait_ge(s_r, 86)
            v.tensor_scalar(lrow[0:1, 9:10], lrow[0:1, 8:9], -1.0, None,
                            ALU.mult).then_inc(s_r, 1)
            v.wait_ge(s_r, 88)
            v.reciprocal(lrow[0:1, 11:12], lrow[0:1, 10:11]).then_inc(s_r, 1)
            v.wait_ge(s_r, 89)
            v.tensor_scalar(lrow[0:1, 0:8], lrow[0:1, 0:8],
                            lrow[0:1, 11:12], None, ALU.mult).then_inc(s_r, 1)
            v.wait_ge(s_r, 91)
            v.tensor_copy(r8b[:], PS[:, 6][0:8, 0:1]).then_inc(s_r, 1)
            v.wait_ge(s_r, 93)
            v.tensor_scalar(rer[:], PS[:, 7][:, 0:1], 2.0, None,
                            ALU.mult).then_inc(s_r, 1)
            v.wait_ge(s_r, 94)
            return None

        def ph2_t(t):
            for c in range(8):
                t.wait_ge(s_r, 1 + 2 * c)
                t.matmul(PS[:, 4 + c % 2][0:1, :], ones8[:],
                         qpb[0:8, c * 512:(c + 1) * 512],
                         start=True, stop=True).then_inc(s_r, 1)
            for c in range(32):
                t.wait_ge(s_r, 17 + 2 * c)
                t.transpose(PS[:, 6 + c % 2][:, 0:1],
                            qrow[0:1, c * 128:(c + 1) * 128],
                            eye[0:1, 0:1]).then_inc(s_r, 1)
            t.wait_ge(s_r, 81)
            for c in range(32):
                last = t.matmul(PS[:, 4][0:8, 0:1], rW_b[:, c, :],
                                qT_b[:, c:c + 1],
                                start=(c == 0), stop=(c == 31))
            last.then_inc(s_r, 1)
            t.wait_ge(s_r, 83)
            t.transpose(PS[:, 5][0:1, 0:8], lg8[:, 0:1], eye[0:8, 0:8]
                        ).then_inc(s_r, 1)
            t.wait_ge(s_r, 90)
            t.transpose(PS[:, 6][0:8, 0:1], lrow[0:1, 0:8], eye[0:1, 0:1]
                        ).then_inc(s_r, 1)
            t.wait_ge(s_r, 92)
            t.matmul(PS[:, 7][:, 0:1], ind8b[:], r8b[:],
                     start=True, stop=True).then_inc(s_r, 1)
            t.wait_ge(s_r, 93)
            return None

        def ph2_sc(s):
            s.wait_ge(s_r, 87)
            s.activation(lrow[0:1, 0:8], lrow[0:1, 0:8], AF.Exp,
                         bias=lrow[0:1, 9:10], scale=1.0,
                         accum_out=lrow[0:1, 10:11]).then_inc(s_r, 1)
            s.wait_ge(s_r, 88)
            return None
        p.phase(vector=ph2_v, tensor=ph2_t, scalar=ph2_sc)

        def ph3_t(t):
            for tc in range(2):
                if tc >= 1:
                    t.wait_ge(s_te, tc)
                for dc in range(32):
                    last = t.matmul(PS[:, tc % 2], A0c[:, dc, :],
                                    x1T[:, dc, tc * 512:(tc + 1) * 512],
                                    start=(dc == 0), stop=(dc == 31))
                last.then_inc(s_tm, 1)
            t.wait_ge(s_tm, 2)
            return None

        def ph3_v(v):
            for tc in range(2):
                v.wait_ge(s_tm, tc + 1)
                v.tensor_scalar(tpT[:, tc * 512:(tc + 1) * 512], PS[:, tc % 2],
                                rer[:, 0:1], None, ALU.mult
                                ).then_inc(s_te, 1)
            v.wait_ge(s_te, 2)
            return None
        p.phase(tensor=ph3_t, vector=ph3_v)

        NMM_OB = 32 * 8 + 8 + 8

        def ph4_g(g):
            for ob in range(8):
                for dc in range(32):
                    j = ob * 32 + dc
                    if j >= 2:
                        ob2, dc2 = divmod(j - 2, 32)
                        g.wait_ge(s_bm, ob2 * 272 + (dc2 + 1) * 8)
                    g.dma_start(out=wch[:, j % 2],
                                in_=W0T[dc * 128:(dc + 1) * 128,
                                        ob * 512:(ob + 1) * 512]
                                ).then_inc(wl0 if j % 2 == 0 else wl1, 16)
            g.wait_ge(wl0, 16 * 128)
            g.wait_ge(wl1, 16 * 128)
            return None

        def ph4_t(t):
            for ob in range(8):
                for dc in range(32):
                    j = ob * 32 + dc
                    t.wait_ge(wl0 if j % 2 == 0 else wl1, (j // 2 + 1) * 16)
                    if dc == 0 and ob > 0:
                        t.wait_ge(s_ze, 8 * ob)
                    for tt in range(8):
                        last = t.matmul(PS[:, tt],
                                        x1T[:, dc, tt * 128:(tt + 1) * 128],
                                        wch[:, j % 2],
                                        start=(dc == 0), stop=False)
                    last.then_inc(s_bm, 8)
                for tt in range(8):
                    t.matmul(PS[:, tt], tpT[:, tt * 128:(tt + 1) * 128],
                             B0f[:, ob * 512:(ob + 1) * 512],
                             start=False, stop=False)
                for tt in range(8):
                    last = t.matmul(PS[:, tt], one1[:],
                                    b0b[0:1, ob * 512:(ob + 1) * 512],
                                    start=False, stop=True)
                last.then_inc(s_bm, 16)
            t.wait_ge(s_bm, NMM_OB * 8)
            return None

        def ph4_sy(sy):
            for ob in range(8):
                for tt in range(8):
                    k = ob * 8 + tt
                    if k >= 2:
                        sy.wait_ge(s_ze, k - 1)
                    sy.dma_start(out=x1c[:, k % 2],
                                 in_=x1_sl[tt * 128:(tt + 1) * 128,
                                           ob * 512:(ob + 1) * 512]
                                 ).then_inc(rl0 if k % 2 == 0 else rl1, 16)
                    sy.wait_ge(s_ze, k + 1)
                    sy.dma_start(out=z_dram[tt * 128:(tt + 1) * 128,
                                            ob * 512:(ob + 1) * 512],
                                 in_=zc[:, k % 2]
                                 ).then_inc(zs0 if k % 2 == 0 else zs1, 16)
            sy.wait_ge(zs0, 16 * 32)
            sy.wait_ge(zs1, 16 * 32)
            return None

        def ph4_v(v):
            for ob in range(8):
                v.wait_ge(s_bm, NMM_OB * (ob + 1))
                for tt in range(8):
                    k = ob * 8 + tt
                    v.wait_ge(rl0 if k % 2 == 0 else rl1, (k // 2 + 1) * 16)
                    if k >= 2:
                        v.wait_ge(zs0 if k % 2 == 0 else zs1, 16 * (k // 2))
                    v.tensor_tensor(zc[:, k % 2], PS[:, tt], x1c[:, k % 2],
                                    ALU.add).then_inc(s_ze, 1)
            v.wait_ge(s_ze, 64)
            return None
        p.phase(gpsimd=ph4_g, tensor=ph4_t, sync=ph4_sy, vector=ph4_v)

        add_ln_phase(p, nc, stack, z_dram, x2_out, "L2", zt=x1t)

        p.finalize()
    return nc


def build_launch_E():
    nc = bass.Bass("TRN2", target_bir_lowering=False, debug=False)
    x2_sl = nc.dram_tensor("x2_sl", [TL, D], F32, kind="ExternalInput").ap()
    qkvw = nc.dram_tensor("qkvw", [D, 3 * D], F32, kind="ExternalInput").ap()
    qkvb = nc.dram_tensor("qkvb", [1, 3 * D], F32, kind="ExternalInput").ap()
    eyef = nc.dram_tensor("eye_f32", [128, 128], F32, kind="ExternalInput").ap()
    qkv_out = nc.dram_tensor("qkv_out", [TL, 3 * D], F32, kind="ExternalOutput").ap()

    NOB = 24

    stack = ExitStack()
    with stack:
        sb = lambda name, shape, dt=F32: stack.enter_context(
            nc.sbuf_tensor(name, shape, dt)).ap()
        psa = lambda name, shape, dt=F32: stack.enter_context(
            nc.psum_tensor(name, shape, dt)).ap()

        PS = psa("PS", [128, 8, 512], F32)
        x2t = sb("x2t", [128, 2, D])
        x2T = sb("x2T", [128, 32, TL], BF16)
        eye = sb("eye", [128, 128], F32)
        one1 = sb("one1", [1, 128], BF16)
        bb = sb("bb", [1, 3 * D], BF16)
        wch = sb("wch", [128, 2, 512], BF16)
        oc = sb("oc", [128, 2, 512])

        p = Prog(nc, stack)
        ld = p.sem("ld")
        ld2 = p.sem("ld2")
        s_ms = p.sem("s_ms")
        xl0, xl1 = p.sem("xl0"), p.sem("xl1")
        s_tp = p.sem("s_tp")
        s_xe = p.sem("s_xe")
        wl0, wl1 = p.sem("wl0"), p.sem("wl1")
        s_bm = p.sem("s_bm")
        s_ze = p.sem("s_ze")
        zs0, zs1 = p.sem("zs0"), p.sem("zs1")

        def ph0_g(g):
            g.dma_start(out=bb[:], in_=qkvb).then_inc(ld, 16)
            g.memset(one1[:], 1.0).then_inc(s_ms, 1)
            g.wait_ge(ld, 16)
            g.wait_ge(s_ms, 1)
            return None

        def ph0_sy(sy):
            sy.dma_start(out=eye[:], in_=eyef).then_inc(ld2, 16)
            sy.wait_ge(ld2, 16)
            return None
        p.phase(gpsimd=ph0_g, sync=ph0_sy)

        def ph1_sy(sy):
            for tt in range(8):
                if tt >= 2:
                    sy.wait_ge(s_tp, (tt - 1) * 32)
                sy.dma_start(out=x2t[:, tt % 2],
                             in_=x2_sl[tt * 128:(tt + 1) * 128, :]
                             ).then_inc(xl0 if tt % 2 == 0 else xl1, 16)
            return None

        def ph1_t(t):
            for tt in range(8):
                t.wait_ge(xl0 if tt % 2 == 0 else xl1, (tt // 2 + 1) * 16)
                for dc in range(32):
                    k = tt * 32 + dc
                    if k >= 2:
                        t.wait_ge(s_xe, k - 1)
                    t.transpose(PS[:, k % 2][:, 0:128],
                                x2t[:, tt % 2, dc * 128:(dc + 1) * 128],
                                eye[:]).then_inc(s_tp, 1)
            t.wait_ge(s_tp, 256)
            return None

        def ph1_v(v):
            for tt in range(8):
                for dc in range(32):
                    k = tt * 32 + dc
                    v.wait_ge(s_tp, k + 1)
                    v.tensor_copy(x2T[:, dc, tt * 128:(tt + 1) * 128],
                                  PS[:, k % 2][:, 0:128]).then_inc(s_xe, 1)
            v.wait_ge(s_xe, 256)
            return None
        p.phase(sync=ph1_sy, tensor=ph1_t, vector=ph1_v)

        NMM_OB = 32 * 8 + 8

        def ph2_g(g):
            for ob in range(NOB):
                for dc in range(32):
                    j = ob * 32 + dc
                    if j >= 2:
                        ob2, dc2 = divmod(j - 2, 32)
                        g.wait_ge(s_bm, ob2 * NMM_OB + (dc2 + 1) * 8)
                    g.dma_start(out=wch[:, j % 2],
                                in_=qkvw[dc * 128:(dc + 1) * 128,
                                         ob * 512:(ob + 1) * 512]
                                ).then_inc(wl0 if j % 2 == 0 else wl1, 16)
            g.wait_ge(wl0, 16 * NOB * 16)
            g.wait_ge(wl1, 16 * NOB * 16)
            return None

        def ph2_t(t):
            for ob in range(NOB):
                for dc in range(32):
                    j = ob * 32 + dc
                    t.wait_ge(wl0 if j % 2 == 0 else wl1, (j // 2 + 1) * 16)
                    if dc == 0 and ob > 0:
                        t.wait_ge(s_ze, 8 * ob)
                    for tt in range(8):
                        last = t.matmul(PS[:, tt],
                                        x2T[:, dc, tt * 128:(tt + 1) * 128],
                                        wch[:, j % 2],
                                        start=(dc == 0), stop=False)
                    last.then_inc(s_bm, 8)
                for tt in range(8):
                    last = t.matmul(PS[:, tt], one1[:],
                                    bb[0:1, ob * 512:(ob + 1) * 512],
                                    start=False, stop=True)
                last.then_inc(s_bm, 8)
            t.wait_ge(s_bm, NMM_OB * NOB)
            return None

        def ph2_sy(sy):
            for ob in range(NOB):
                for tt in range(8):
                    k = ob * 8 + tt
                    sy.wait_ge(s_ze, k + 1)
                    sy.dma_start(out=qkv_out[tt * 128:(tt + 1) * 128,
                                             ob * 512:(ob + 1) * 512],
                                 in_=oc[:, k % 2]
                                 ).then_inc(zs0 if k % 2 == 0 else zs1, 16)
            sy.wait_ge(zs0, 16 * NOB * 4)
            sy.wait_ge(zs1, 16 * NOB * 4)
            return None

        def ph2_v(v):
            for ob in range(NOB):
                v.wait_ge(s_bm, NMM_OB * (ob + 1))
                for tt in range(8):
                    k = ob * 8 + tt
                    if k >= 2:
                        v.wait_ge(zs0 if k % 2 == 0 else zs1, 16 * (k // 2))
                    v.tensor_copy(oc[:, k % 2], PS[:, tt]).then_inc(s_ze, 1)
            v.wait_ge(s_ze, 8 * NOB)
            return None
        p.phase(gpsimd=ph2_g, tensor=ph2_t, sync=ph2_sy, vector=ph2_v)

        p.finalize()
    return nc


def build_launch_F():
    nc = bass.Bass("TRN2", target_bir_lowering=False, debug=False)
    qkv_sl = nc.dram_tensor("qkv_sl", [TL, 3 * D], F32, kind="ExternalInput").ap()
    KT_full = nc.dram_tensor("KT_full", [D, T], F32, kind="ExternalInput").ap()
    V_full = nc.dram_tensor("V_full", [T, D], F32, kind="ExternalInput").ap()
    x2_sl = nc.dram_tensor("x2_sl", [TL, D], F32, kind="ExternalInput").ap()
    ow_i = nc.dram_tensor("ow", [D, D], F32, kind="ExternalInput").ap()
    ob_i = nc.dram_tensor("ob", [1, D], F32, kind="ExternalInput").ap()
    r1WT = nc.dram_tensor("r1WT", [D, 8], F32, kind="ExternalInput").ap()
    A1catT = nc.dram_tensor("A1catT", [D, 128], F32, kind="ExternalInput").ap()
    B1flatT = nc.dram_tensor("B1flatT", [128, D], F32, kind="ExternalInput").ap()
    eyef = nc.dram_tensor("eye_f32", [128, 128], F32, kind="ExternalInput").ap()
    out_sl = nc.dram_tensor("out_sl", [TL, D], F32, kind="ExternalOutput").ap()
    z3_dram = nc.dram_tensor("z3_dram", [TL, D], F32).ap()
    x3_dram = nc.dram_tensor("x3_dram", [TL, D], F32).ap()
    z4_dram = nc.dram_tensor("z4_dram", [TL, D], F32).ap()
    ncd = nc.dram_tensor("ncd", [1, 32], F32).ap()

    SCL = 1.0 / float(np.sqrt(HD))

    stack = ExitStack()
    with stack:
        sb = lambda name, shape, dt=F32: stack.enter_context(
            nc.sbuf_tensor(name, shape, dt)).ap()
        psa = lambda name, shape, dt=F32: stack.enter_context(
            nc.psum_tensor(name, shape, dt)).ap()

        PS = psa("PS", [128, 8, 512], F32)
        QT = sb("QT", [128, 32, TL], BF16)
        ctxT = sb("ctxT", [128, 32, TL], BF16)
        xqt = sb("xqt", [128, 1, D])
        KT = sb("KT", [128, T], BF16)
        Vb = sb("Vb", [128, 64, 128], BF16)
        k2s = sb("k2s", [128, 2, 512], BF16)
        pch = sb("pch", [128, 2, 2, 512], BF16)
        eye = sb("eye", [128, 128], F32)
        onesb = sb("onesb", [128, 1], BF16)
        onesf = sb("onesf", [1, 128])
        one1 = sb("one1", [1, 128], BF16)
        obb = sb("obb", [1, D], BF16)
        mq = sb("mq", [1, 32])
        krow = sb("krow", [1, 24])
        negCb = sb("negCb", [128, 2])
        wch = sb("wch", [128, 2, 512], BF16)
        x2c = sb("x2c", [128, 2, 512])
        zc = sb("zc", [128, 2, 512])
        r1b = sb("r1b", [128, 32, 8], BF16)
        A1c = KT[:, 0:D].rearrange("p (c f) -> p c f", f=128)
        B1f = KT[:, D:2 * D]
        ind8b = sb("ind8b", [8, 128], BF16)
        lg = sb("lg", [128, 2, 24])
        w8Tb = Vb[0:8, 0:8, :].rearrange("p c f -> p (c f)")
        hTb = Vb[:, 8:16, :].rearrange("p c f -> p (c f)")
        whT = Vb[:, 16:24, :].rearrange("p c f -> p (c f)")

        ind8_i = nc.dram_tensor("ind8", [8, 128], F32, kind="ExternalInput").ap()

        ctxf = xqt[:, 0, 0:TL]
        linv = xqt[0:1, 0, TL:2 * TL]

        p = Prog(nc, stack)
        ld = p.sem("ld")
        ld2 = p.sem("ld2")
        s_ms = p.sem("s_ms")
        xl0, xl1 = p.sem("xl0"), p.sem("xl1")
        s_tp = p.sem("s_tp")
        s_xe = p.sem("s_xe")
        s_qs = p.sem("s_qs")
        kl = p.sem("kl")
        vl = p.sem("vl")
        s_kq = p.sem("s_kq")
        s_km = p.sem("s_km")
        s_ke = p.sem("s_ke")
        s_kx = p.sem("s_kx")
        s_ns = p.sem("s_ns")
        s_nl = p.sem("s_nl")
        s_sm = p.sem("s_sm")
        s_ex = p.sem("s_ex")
        s_pv = p.sem("s_pv")
        s_lm = p.sem("s_lm")
        s_ce = p.sem("s_ce")
        s_li = p.sem("s_li")
        s_cb = p.sem("s_cb")
        s_cc = p.sem("s_cc")

        def ph0_g(g):
            g.dma_start(out=obb[:], in_=ob_i).then_inc(ld, 16)
            g.dma_start(out=r1b[:], in_=r1WT.rearrange("(c p) r -> p c r", p=128)
                        ).then_inc(ld, 16)
            g.dma_start(out=ind8b[:], in_=ind8_i).then_inc(ld, 16)
            g.memset(onesb[:], 1.0).then_inc(s_ms, 1)
            g.memset(onesf[:], 1.0).then_inc(s_ms, 1)
            g.memset(one1[:], 1.0).then_inc(s_ms, 1)
            g.wait_ge(ld, 48)
            g.wait_ge(s_ms, 3)
            return None

        def ph0_sy(sy):
            sy.dma_start(out=eye[:], in_=eyef).then_inc(ld2, 16)
            sy.wait_ge(ld2, 16)
            return None
        p.phase(gpsimd=ph0_g, sync=ph0_sy)

        def ph1_sy(sy):
            for tt in range(8):
                if tt >= 1:
                    sy.wait_ge(s_tp, tt * 32)
                sy.dma_start(out=xqt[:, 0],
                             in_=qkv_sl[tt * 128:(tt + 1) * 128, 0:D]
                             ).then_inc(xl0 if tt % 2 == 0 else xl1, 16)
            return None

        def ph1_t(t):
            for tt in range(8):
                t.wait_ge(xl0 if tt % 2 == 0 else xl1, (tt // 2 + 1) * 16)
                for dc in range(32):
                    k = tt * 32 + dc
                    if k >= 2:
                        t.wait_ge(s_xe, k - 1)
                    t.transpose(PS[:, k % 2][:, 0:128],
                                xqt[:, 0, dc * 128:(dc + 1) * 128],
                                eye[:]).then_inc(s_tp, 1)
            t.wait_ge(s_tp, 256)
            return None

        def ph1_v(v):
            for tt in range(8):
                for dc in range(32):
                    k = tt * 32 + dc
                    v.wait_ge(s_tp, k + 1)
                    v.tensor_copy(QT[:, dc, tt * 128:(tt + 1) * 128],
                                  PS[:, k % 2][:, 0:128]).then_inc(s_xe, 1)
            v.wait_ge(s_xe, 256)
            return None
        p.phase(sync=ph1_sy, tensor=ph1_t, vector=ph1_v)

        def ph2_sc(s):
            for h in range(32):
                b = 7 * h
                s.wait_ge(s_qs, b)
                s.activation(k2s[:, 0], QT[:, h, 0:512], AF.Square, bias=0.0,
                             scale=1.0)
                s.activation(k2s[:, 1], QT[:, h, 512:1024], AF.Square, bias=0.0,
                             scale=1.0).then_inc(s_qs, 1)
                s.wait_ge(s_qs, b + 6)
                s.activation(mq[0:1, h:h + 1], krow[0:1, 20:21], AF.Sqrt,
                             bias=0.0, scale=1.0).then_inc(s_qs, 1)
            s.wait_ge(s_qs, 224)
            return None

        def ph2_t(t):
            for h in range(32):
                b = 7 * h
                for qc in range(2):
                    t.wait_ge(s_qs, b + 1)
                    if h >= 1:
                        t.wait_ge(s_qs, b - 7 + 4 + qc)
                    t.matmul(PS[:, 6 + qc][0:1, :], onesb[:],
                             k2s[:, qc],
                             start=True, stop=True).then_inc(s_qs, 1)
            return None

        def ph2_v(v):
            for h in range(32):
                b = 7 * h
                v.wait_ge(s_qs, b + 3)
                v.reduce_max(krow[0:1, 21:22], PS[:, 6][0:1, :],
                             AX).then_inc(s_qs, 1)
                v.wait_ge(s_qs, b + 4)
                v.reduce_max(krow[0:1, 22:23], PS[:, 7][0:1, :],
                             AX).then_inc(s_qs, 1)
                v.wait_ge(s_qs, b + 5)
                v.tensor_tensor(krow[0:1, 20:21], krow[0:1, 21:22],
                                krow[0:1, 22:23], ALU.max).then_inc(s_qs, 1)
            v.wait_ge(s_qs, 223)
            return None
        p.phase(scalar=ph2_sc, tensor=ph2_t, vector=ph2_v)

        PK = {"kl": 16, "vl": 16, "kq": 16, "km": 16, "ke": 16, "kx": 4,
              "ns": 16, "nl": 16, "sm": 128, "ex": 128, "pv": 128, "lm": 128,
              "ce": 2, "li": 2, "cb": 2, "cc": 2}

        def ph3_g(g):
            for h in range(32):
                if h >= 1:
                    g.wait_ge(s_sm, 128 * h)
                    g.wait_ge(s_kq, 16 * h)
                    g.wait_ge(s_pv, 128 * h)
                g.dma_start(out=KT[:],
                            in_=KT_full[h * 128:(h + 1) * 128, :]
                            ).then_inc(kl, 16)
                g.dma_start(out=Vb[:],
                            in_=V_full.rearrange("(c p) f -> p c f", p=128)
                            [:, :, h * 128:(h + 1) * 128]
                            ).then_inc(vl, 16)
            g.wait_ge(kl, 512)
            g.wait_ge(vl, 512)
            return None

        def ph3_sc(s):
            for h in range(32):
                bq = 16 * h
                bx = 4 * h
                be = 128 * h
                for i in range(16):
                    s.wait_ge(kl, 16 * (h + 1))
                    if i >= 2:
                        s.wait_ge(s_km, 16 * h + i - 1)
                    s.activation(k2s[:, i % 2], KT[:, i * 512:(i + 1) * 512],
                                 AF.Square, bias=0.0, scale=1.0
                                 ).then_inc(s_kq, 1)
                s.wait_ge(s_kx, bx + 1)
                s.activation(krow[0:1, 17:18], krow[0:1, 16:17], AF.Sqrt,
                             bias=0.0, scale=1.0).then_inc(s_kx, 1)
                for kc in range(64):
                    for qc in range(2):
                        s.wait_ge(s_sm, be + 2 * kc + qc + 1)
                        s.wait_ge(s_nl, 16 * (h + 1))
                        if kc >= 2:
                            s.wait_ge(s_pv, be + 2 * (kc - 2) + 2)
                        s.activation(pch[:, kc % 2, qc], PS[:, 6 + qc],
                                     AF.Exp, bias=negCb[:, h % 2:h % 2 + 1],
                                     scale=SCL).then_inc(s_ex, 1)
            s.wait_ge(s_ex, 32 * 128)
            return None

        def ph3_t(t):
            for h in range(32):
                bq = 16 * h
                be = 128 * h
                for i in range(16):
                    t.wait_ge(s_kq, 16 * h + i + 1)
                    if i >= 2:
                        t.wait_ge(s_ke, 16 * h + i - 1)
                    if i < 2 and h >= 1:
                        t.wait_ge(s_cc, 2 * h)
                    t.matmul(PS[:, 4 + i % 2][0:1, :], onesb[:],
                             k2s[:, i % 2], start=True, stop=True
                             ).then_inc(s_km, 1)
                for kc in range(64):
                    for qc in range(2):
                        if kc >= 1:
                            t.wait_ge(s_ex, be + 2 * (kc - 1) + qc + 1)
                        if kc == 0 and qc == 0:
                            t.wait_ge(kl, 16 * (h + 1))
                        if kc == 0 and h >= 1:
                            t.wait_ge(s_ce, 2 * h)
                            t.wait_ge(s_li, 2 * h)
                            t.wait_ge(s_ex, 128 * h)
                        t.matmul(PS[:, 6 + qc], KT[:, kc * 128:(kc + 1) * 128],
                                 QT[:, h, qc * 512:(qc + 1) * 512],
                                 start=True, stop=True).then_inc(s_sm, 1)
                    if kc >= 1:
                        if kc == 1:
                            t.wait_ge(vl, 16 * (h + 1))
                        for qc in range(2):
                            t.wait_ge(s_ex, be + 2 * (kc - 1) + qc + 1)
                            t.matmul(PS[:, qc], Vb[:, kc - 1, :],
                                     pch[:, (kc - 1) % 2, qc],
                                     start=(kc == 1), stop=(kc == 64)
                                     ).then_inc(s_pv, 1)
                            t.matmul(PS[:, 2 + qc][0:1, :], onesb[:],
                                     pch[:, (kc - 1) % 2, qc],
                                     start=(kc == 1), stop=(kc == 64)
                                     ).then_inc(s_lm, 1)
                for qc in range(2):
                    t.wait_ge(s_ex, be + 2 * 63 + qc + 1)
                    t.matmul(PS[:, qc], Vb[:, 63, :], pch[:, 63 % 2, qc],
                             start=False, stop=True).then_inc(s_pv, 1)
                    t.matmul(PS[:, 2 + qc][0:1, :], onesb[:],
                             pch[:, 63 % 2, qc],
                             start=False, stop=True).then_inc(s_lm, 1)
                for qc in range(2):
                    t.wait_ge(s_li, 2 * h + qc + 1)
                    t.matmul(PS[:, 4 + qc], onesf[:],
                             linv[0:1, qc * 512:(qc + 1) * 512],
                             start=True, stop=True).then_inc(s_cb, 1)
            t.wait_ge(s_cb, 64)
            return None

        def ph3_v(v):
            for h in range(32):
                bq = 16 * h
                bx = 4 * h
                for i in range(16):
                    v.wait_ge(s_km, 16 * h + i + 1)
                    v.reduce_max(krow[0:1, i:i + 1], PS[:, 4 + i % 2][0:1, :],
                                 AX).then_inc(s_ke, 1)
                v.wait_ge(s_ke, 16 * (h + 1))
                v.reduce_max(krow[0:1, 16:17], krow[0:1, 0:16],
                             AX).then_inc(s_kx, 1)
                v.wait_ge(s_kx, bx + 2)
                v.tensor_tensor(krow[0:1, 18:19], krow[0:1, 17:18],
                                mq[0:1, h:h + 1], ALU.mult).then_inc(s_kx, 1)
                v.wait_ge(s_kx, bx + 3)
                v.tensor_scalar(krow[0:1, 19:20], krow[0:1, 18:19],
                                -SCL, None, ALU.mult).then_inc(s_kx, 1)
                v.wait_ge(s_pv, 128 * (h + 1))
                v.wait_ge(s_lm, 128 * (h + 1))
                for qc in range(2):
                    v.tensor_copy(ctxf[:, qc * 512:(qc + 1) * 512],
                                  PS[:, qc]).then_inc(s_ce, 1)
                    v.wait_ge(s_ce, 2 * h + qc + 1)
                    v.reciprocal(linv[0:1, qc * 512:(qc + 1) * 512],
                                 PS[:, 2 + qc][0:1, :]).then_inc(s_li, 1)
                for qc in range(2):
                    v.wait_ge(s_cb, 2 * h + qc + 1)
                    v.tensor_tensor(ctxT[:, h, qc * 512:(qc + 1) * 512],
                                    ctxf[:, qc * 512:(qc + 1) * 512],
                                    PS[:, 4 + qc], ALU.mult).then_inc(s_cc, 1)
            v.wait_ge(s_cc, 64)
            return None

        def ph3_sy(sy):
            for h in range(32):
                bx = 4 * h
                sy.wait_ge(s_kx, bx + 4)
                sy.dma_start(out=ncd[0:1, h:h + 1], in_=krow[0:1, 19:20],
                             allow_slow_non_contiguous=True).then_inc(s_ns, 16)
                sy.wait_ge(s_ns, 16 * (h + 1))
                if h >= 2:
                    sy.wait_ge(s_ex, 128 * (h - 1))
                sy.dma_start(out=negCb[:, h % 2:h % 2 + 1],
                             in_=ncd[0:1, h:h + 1].partition_broadcast(128).opt(),
                             allow_slow_non_contiguous=True).then_inc(s_nl, 16)
            sy.wait_ge(s_nl, 512)
            return None
        p.phase(gpsimd=ph3_g, scalar=ph3_sc, tensor=ph3_t, vector=ph3_v,
                sync=ph3_sy)

        s_bm = p.sem("s_bm")
        wl0, wl1 = p.sem("wl0"), p.sem("wl1")
        rl0, rl1 = p.sem("rl0"), p.sem("rl1")
        s_ze = p.sem("s_ze")
        zs0, zs1 = p.sem("zs0"), p.sem("zs1")
        NMM_OB = 32 * 8 + 8

        def ph4_g(g):
            for ob in range(8):
                for hh in range(32):
                    j = ob * 32 + hh
                    if j >= 2:
                        ob2, h2 = divmod(j - 2, 32)
                        g.wait_ge(s_bm, ob2 * NMM_OB + (h2 + 1) * 8)
                    g.dma_start(out=wch[:, j % 2],
                                in_=ow_i[hh * 128:(hh + 1) * 128,
                                         ob * 512:(ob + 1) * 512]
                                ).then_inc(wl0 if j % 2 == 0 else wl1, 16)
            g.wait_ge(wl0, 16 * 128)
            g.wait_ge(wl1, 16 * 128)
            return None

        def ph4_t(t):
            for ob in range(8):
                for hh in range(32):
                    j = ob * 32 + hh
                    t.wait_ge(wl0 if j % 2 == 0 else wl1, (j // 2 + 1) * 16)
                    if hh == 0 and ob > 0:
                        t.wait_ge(s_ze, 8 * ob)
                    for tt in range(8):
                        last = t.matmul(PS[:, tt],
                                        ctxT[:, hh, tt * 128:(tt + 1) * 128],
                                        wch[:, j % 2],
                                        start=(hh == 0), stop=False)
                    last.then_inc(s_bm, 8)
                for tt in range(8):
                    last = t.matmul(PS[:, tt], one1[:],
                                    obb[0:1, ob * 512:(ob + 1) * 512],
                                    start=False, stop=True)
                last.then_inc(s_bm, 8)
            t.wait_ge(s_bm, NMM_OB * 8)
            return None

        def ph4_sy(sy):
            for ob in range(8):
                for tt in range(8):
                    k = ob * 8 + tt
                    if k >= 2:
                        sy.wait_ge(s_ze, k - 1)
                    sy.dma_start(out=x2c[:, k % 2],
                                 in_=x2_sl[tt * 128:(tt + 1) * 128,
                                           ob * 512:(ob + 1) * 512]
                                 ).then_inc(rl0 if k % 2 == 0 else rl1, 16)
                    sy.wait_ge(s_ze, k + 1)
                    sy.dma_start(out=z3_dram[tt * 128:(tt + 1) * 128,
                                             ob * 512:(ob + 1) * 512],
                                 in_=zc[:, k % 2]
                                 ).then_inc(zs0 if k % 2 == 0 else zs1, 16)
            sy.wait_ge(zs0, 16 * 32)
            sy.wait_ge(zs1, 16 * 32)
            return None

        def ph4_v(v):
            for ob in range(8):
                v.wait_ge(s_bm, NMM_OB * (ob + 1))
                for tt in range(8):
                    k = ob * 8 + tt
                    v.wait_ge(rl0 if k % 2 == 0 else rl1, (k // 2 + 1) * 16)
                    if k >= 2:
                        v.wait_ge(zs0 if k % 2 == 0 else zs1, 16 * (k // 2))
                    v.tensor_tensor(zc[:, k % 2], PS[:, tt], x2c[:, k % 2],
                                    ALU.add).then_inc(s_ze, 1)
            v.wait_ge(s_ze, 64)
            return None
        p.phase(gpsimd=ph4_g, tensor=ph4_t, sync=ph4_sy, vector=ph4_v)

        ot_v = QT[:, 0:8, :].rearrange("p c f -> p (c f)").bitcast(F32).unsqueeze(1)
        sq_v = QT[:, 8:16, :].rearrange("p c f -> p (c f)").rearrange(
            "p (b f) -> p b f", b=2)
        srow_v = sb("lnsrow", [128, 2, 8])
        lnbufs = (ot_v, sq_v, srow_v)
        add_ln_phase(p, nc, stack, z3_dram, x3_dram, "L3", zt=xqt, bufs=lnbufs)

        x3T = QT
        xl2, xl3 = p.sem("xl2"), p.sem("xl3")
        s_tp2 = p.sem("s_tp2")
        s_xe2 = p.sem("s_xe2")

        lf = p.sem("lf")

        def ph5_g(g):
            g.dma_start(out=A1c, in_=A1catT.rearrange("(c p) r -> p c r", p=128)
                        ).then_inc(lf, 16)
            g.dma_start(out=B1f, in_=B1flatT).then_inc(lf, 16)
            g.wait_ge(lf, 32)
            return None

        def ph5_sy(sy):
            for tt in range(8):
                if tt >= 1:
                    sy.wait_ge(s_tp2, tt * 32)
                sy.dma_start(out=xqt[:, 0],
                             in_=x3_dram[tt * 128:(tt + 1) * 128, :]
                             ).then_inc(xl2 if tt % 2 == 0 else xl3, 16)
            return None

        def ph5_t(t):
            for tt in range(8):
                t.wait_ge(xl2 if tt % 2 == 0 else xl3, (tt // 2 + 1) * 16)
                for dc in range(32):
                    k = tt * 32 + dc
                    if k >= 2:
                        t.wait_ge(s_xe2, k - 1)
                    t.transpose(PS[:, k % 2][:, 0:128],
                                xqt[:, 0, dc * 128:(dc + 1) * 128],
                                eye[:]).then_inc(s_tp2, 1)
            t.wait_ge(s_tp2, 256)
            return None

        def ph5_v(v):
            for tt in range(8):
                for dc in range(32):
                    k = tt * 32 + dc
                    v.wait_ge(s_tp2, k + 1)
                    v.tensor_copy(x3T[:, dc, tt * 128:(tt + 1) * 128],
                                  PS[:, k % 2][:, 0:128]).then_inc(s_xe2, 1)
            v.wait_ge(s_xe2, 256)
            return None
        p.phase(sync=ph5_sy, tensor=ph5_t, vector=ph5_v, gpsimd=ph5_g)

        s_lgm = p.sem("s_lgm")
        s_lg = p.sem("s_lg")
        s_w8 = p.sem("s_w8")
        s_wt = p.sem("s_wt")
        s_we = p.sem("s_we")
        s_wm = p.sem("s_wm")
        s_hm = p.sem("s_hm")
        s_he = p.sem("s_he")
        s_wh = p.sem("s_wh")

        def ph6_t(t):
            for tt in range(10):
                if tt < 8:
                    if tt >= 2:
                        t.wait_ge(s_lg, 4 * (tt - 2) + 1)
                    for dc in range(32):
                        last = t.matmul(PS[:, tt % 2][:, 0:8], x3T[:, dc,
                                        tt * 128:(tt + 1) * 128],
                                        r1b[:, dc, :],
                                        start=(dc == 0), stop=(dc == 31))
                    last.then_inc(s_lgm, 1)
                if tt >= 2:
                    k = tt - 2
                    t.wait_ge(s_w8, k + 1)
                    if k >= 2:
                        t.wait_ge(s_lg, 4 * (k - 2) + 4)
                    t.transpose(PS[:, 2 + k % 2][0:8, 0:128], lg[:, k % 2, 8:16],
                                eye[:]).then_inc(s_wt, 1)
            for tc in range(2):
                t.wait_ge(s_we, 8)
                t.matmul(PS[:, 4 + tc], ind8b[:],
                         w8Tb[0:8, tc * 512:(tc + 1) * 512],
                         start=True, stop=True).then_inc(s_wm, 1)
            for tc in range(2):
                for dc in range(32):
                    last = t.matmul(PS[:, 6 + tc], A1c[:, dc, :],
                                    x3T[:, dc, tc * 512:(tc + 1) * 512],
                                    start=(dc == 0), stop=(dc == 31))
                last.then_inc(s_hm, 1)
            t.wait_ge(s_wm, 2)
            t.wait_ge(s_hm, 2)
            return None

        def ph6_v(v):
            for tt in range(8):
                v.wait_ge(s_lgm, tt + 1)
                v.tensor_copy(lg[:, tt % 2, 0:8],
                              PS[:, tt % 2][:, 0:8]).then_inc(s_lg, 1)
                v.wait_ge(s_lg, 4 * tt + 1)
                v.reduce_max(lg[:, tt % 2, 16:17], lg[:, tt % 2, 0:8],
                             AX).then_inc(s_lg, 1)
                v.wait_ge(s_lg, 4 * tt + 2)
                v.tensor_scalar(lg[:, tt % 2, 8:16], lg[:, tt % 2, 0:8],
                                lg[:, tt % 2, 16:17], None,
                                ALU.is_ge).then_inc(s_lg, 1)
                v.wait_ge(s_lg, 4 * tt + 3)
                v.tensor_tensor(lg[:, tt % 2, 8:16], lg[:, tt % 2, 8:16],
                                lg[:, tt % 2, 0:8], ALU.mult).then_inc(s_w8, 1)
                v.wait_ge(s_wt, tt + 1)
                v.tensor_copy(w8Tb[0:8, tt * 128:(tt + 1) * 128],
                              PS[:, 2 + tt % 2][0:8, 0:128]).then_inc(s_lg, 1)
            v.wait_ge(s_lg, 32)
            v.tensor_copy(lg[:, 0, 0:1], lg[:, 0, 0:1]).then_inc(s_we, 8)
            for tc in range(2):
                v.wait_ge(s_hm, tc + 1)
                v.tensor_copy(hTb[:, tc * 512:(tc + 1) * 512],
                              PS[:, 6 + tc]).then_inc(s_he, 1)
                v.wait_ge(s_he, tc + 1)
                v.wait_ge(s_wm, tc + 1)
                v.tensor_tensor(whT[:, tc * 512:(tc + 1) * 512],
                                hTb[:, tc * 512:(tc + 1) * 512],
                                PS[:, 4 + tc], ALU.mult).then_inc(s_wh, 1)
            v.wait_ge(s_wh, 2)
            return None
        p.phase(tensor=ph6_t, vector=ph6_v)

        s_bm2 = p.sem("s_bm2")
        rl2, rl3 = p.sem("rl2"), p.sem("rl3")
        s_ze2 = p.sem("s_ze2")
        zs2, zs3 = p.sem("zs2"), p.sem("zs3")

        def ph7_t(t):
            for ob in range(8):
                if ob >= 1:
                    t.wait_ge(s_ze2, 8 * ob)
                for tt in range(8):
                    t.matmul(PS[:, tt], whT[:, tt * 128:(tt + 1) * 128],
                             B1f[:, ob * 512:(ob + 1) * 512],
                             start=True, stop=True).then_inc(s_bm2, 1)
            t.wait_ge(s_bm2, 64)
            return None

        def ph7_sy(sy):
            for ob in range(8):
                for tt in range(8):
                    k = ob * 8 + tt
                    if k >= 2:
                        sy.wait_ge(s_ze2, k - 1)
                    sy.dma_start(out=x2c[:, k % 2],
                                 in_=x3_dram[tt * 128:(tt + 1) * 128,
                                             ob * 512:(ob + 1) * 512]
                                 ).then_inc(rl2 if k % 2 == 0 else rl3, 16)
                    sy.wait_ge(s_ze2, k + 1)
                    sy.dma_start(out=z4_dram[tt * 128:(tt + 1) * 128,
                                             ob * 512:(ob + 1) * 512],
                                 in_=zc[:, k % 2]
                                 ).then_inc(zs2 if k % 2 == 0 else zs3, 16)
            sy.wait_ge(zs2, 16 * 32)
            sy.wait_ge(zs3, 16 * 32)
            return None

        def ph7_v(v):
            for ob in range(8):
                for tt in range(8):
                    k = ob * 8 + tt
                    v.wait_ge(s_bm2, k + 1)
                    v.wait_ge(rl2 if k % 2 == 0 else rl3, (k // 2 + 1) * 16)
                    if k >= 2:
                        v.wait_ge(zs2 if k % 2 == 0 else zs3, 16 * (k // 2))
                    v.tensor_tensor(zc[:, k % 2], PS[:, tt], x2c[:, k % 2],
                                    ALU.add).then_inc(s_ze2, 1)
            v.wait_ge(s_ze2, 64)
            return None
        p.phase(tensor=ph7_t, sync=ph7_sy, vector=ph7_v)

        add_ln_phase(p, nc, stack, z4_dram, out_sl, "L4", zt=xqt, bufs=lnbufs)

        p.finalize()
    return nc


_NC_CACHE = {}


def _get_nc(name, builder):
    if name not in _NC_CACHE:
        _NC_CACHE[name] = builder()
    return _NC_CACHE[name]


def kernel(**inputs):
    d = _prep(inputs)
    cores = list(range(NCORES))

    ncA = _get_nc("A", build_launch_A)
    maps = [{"x_sl": d["x_full"][c * TL:(c + 1) * TL], "xpwT": d["xpwT"],
             "eye_f32": d["eye_f32"]} for c in cores]
    res = run_bass_kernel_spmd(ncA, maps, core_ids=cores)
    xpT_full = np.ascontiguousarray(
        np.concatenate([r["xpT_out"] for r in res.results], axis=1))

    ncB = _get_nc("B", build_launch_B)
    maps = [{"x_cols": d["x_cols"][c], "xpT_full": xpT_full,
             "Alog_sl": d["Alog_sl"][c], "D_sl": d["D_sl"][c],
             "dtpwT_sl": d["dtpwT_sl"][c], "dtpb_sl": d["dtpb_sl"][c],
             "eye_f32": d["eye_f32"]} for c in cores]
    res = run_bass_kernel_spmd(ncB, maps, core_ids=cores)
    ymix = np.ascontiguousarray(
        np.concatenate([r["ymixT"] for r in res.results], axis=0).T)

    ncC = _get_nc("C", build_launch_C)
    maps = [{"x_sl": d["x_full"][c * TL:(c + 1) * TL],
             "ymix_sl": ymix[c * TL:(c + 1) * TL],
             "qmask": d["qmask"][c]} for c in cores]
    res = run_bass_kernel_spmd(ncC, maps, core_ids=cores)
    x1 = [np.ascontiguousarray(r["x1_out"]) for r in res.results]
    qparts = np.ascontiguousarray(
        np.concatenate([r["qp_out"] for r in res.results], axis=0))

    ncD = _get_nc("D", build_launch_D)
    maps = [{"x1_sl": x1[c], "qparts": qparts, "W0T": d["W0T"], "b0": d["b0"],
             "A0catT": d["A0catT"], "B0flatT": d["B0flatT"], "rWT": d["rWT"],
             "rb": d["rb"], "ind8": d["ind8"], "eye_f32": d["eye_f32"]}
            for c in cores]
    res = run_bass_kernel_spmd(ncD, maps, core_ids=cores)
    x2 = [np.ascontiguousarray(r["x2_out"]) for r in res.results]

    ncE = _get_nc("E", build_launch_E)
    maps = [{"x2_sl": x2[c], "qkvw": d["qkvw"], "qkvb": d["qkvb"],
             "eye_f32": d["eye_f32"]} for c in cores]
    res = run_bass_kernel_spmd(ncE, maps, core_ids=cores)
    qkv = [np.ascontiguousarray(r["qkv_out"]) for r in res.results]
    qkv_full = np.concatenate(qkv, axis=0)
    KT_full = np.ascontiguousarray(qkv_full[:, D:2 * D].T)
    V_full = np.ascontiguousarray(qkv_full[:, 2 * D:3 * D])

    ncF = _get_nc("F", build_launch_F)
    maps = [{"qkv_sl": qkv[c], "KT_full": KT_full, "V_full": V_full,
             "x2_sl": x2[c], "ow": d["ow"], "ob": d["ob"], "r1WT": d["r1WT"],
             "A1catT": d["A1catT"], "B1flatT": d["B1flatT"],
             "ind8": d["ind8"], "eye_f32": d["eye_f32"]} for c in cores]
    res = run_bass_kernel_spmd(ncF, maps, core_ids=cores)
    out = np.concatenate([r["out_sl"] for r in res.results], axis=0)
    return out.reshape(1, T, D).astype(np.float32)
